# version 41
# baseline (speedup 1.0000x reference)
import contextlib
import numpy as np
import ml_dtypes
import concourse.bass as bass
import concourse.mybir as mybir
from concourse.bass_utils import run_bass_kernel_spmd

F32 = mybir.dt.float32
BF16 = mybir.dt.bfloat16
U8 = mybir.dt.uint8
AF = mybir.ActivationFunctionType
ALU = mybir.AluOpType
AX = mybir.AxisListType

T = 2048
C = 1024
TT = 512
NTILE = T // TT
NCH = TT // 128
F1 = 4096
F2 = 4096
EPS = 1e-5

ROLES = ("sync", "tensor", "vector", "scalar", "gpsimd")
NPOOL = 12
SEMS = (("pe", "ve", "ae", "gx")
        + tuple(f"qs{i}" for i in range(NPOOL))
        + tuple(f"qg{i}" for i in range(NPOOL)))
_DEBUG = False
_DEBUG_W = None


class Ctx:

    def __init__(self, role, eng, sems):
        self.role = role
        self.eng = eng
        self.sems = sems
        self.cnt = {k: 0 for k in SEMS}
        self.res = {}
        self.waited = {k: 0 for k in SEMS}
        self.dma_idx = {"qs": 0, "qg": 0}

    def emit(self, owner, build_fn, inc_sem, inc_n=1, reads=(), writes=()):
        deps = []
        if inc_sem in ("qs", "qg"):
            q = inc_sem
            inc_sem = f"{q}{self.dma_idx[q] % NPOOL}"
            self.dma_idx[q] += 1
            if self.cnt[inc_sem] > 0:
                deps.append((inc_sem, self.cnt[inc_sem]))
        for r in reads:
            e = self.res.get(r)
            if e and e["w"]:
                deps.append(e["w"])
        for w in writes:
            e = self.res.get(w)
            if e:
                deps.extend(e["r"])
                if e["w"]:
                    deps.append(e["w"])
        if self.role == owner:
            merged = {}
            for s, v in deps:
                if owner == "tensor" and s == "pe":
                    continue
                merged[s] = max(v, merged.get(s, 0))
            for s, v in sorted(merged.items()):
                if v > self.waited[s]:
                    self.eng.wait_ge(self.sems[s], v)
                    self.waited[s] = v
            inst = build_fn(self.eng)
            inst.then_inc(self.sems[inc_sem], inc_n)
        self.cnt[inc_sem] += inc_n
        tok = (inc_sem, self.cnt[inc_sem])
        for r in reads:
            self.res.setdefault(r, {"w": None, "r": []})["r"].append(tok)
        for w in writes:
            self.res[w] = {"w": tok, "r": []}
        return tok


def build_nc():
    nc = bass.Bass("TRN2", target_bir_lowering=False)

    d_x = nc.dram_tensor("x", [T, C], F32, kind="ExternalInput")
    d_k0 = nc.dram_tensor("k0", [2 * C, F1], U8, kind="ExternalInput")
    d_k1 = nc.dram_tensor("k1", [F1, F2], U8, kind="ExternalInput")
    d_k2 = nc.dram_tensor("k2", [F2, C], U8, kind="ExternalInput")
    d_sv = nc.dram_tensor("svals", [1, 3], F32, kind="ExternalInput")
    d_ident = nc.dram_tensor("ident", [128, 128], BF16, kind="ExternalInput")
    d_triu = nc.dram_tensor("triu", [128, 128], BF16, kind="ExternalInput")
    d_ones_sq = nc.dram_tensor("ones_sq", [128, 128], BF16, kind="ExternalInput")
    d_ones_row = nc.dram_tensor("ones_row", [1, 128], BF16, kind="ExternalInput")
    d_ones_col = nc.dram_tensor("ones_col", [128, 1], BF16, kind="ExternalInput")
    d_ones_f32 = nc.dram_tensor("ones_f32", [1, 128], F32, kind="ExternalInput")

    d_w0s = nc.dram_tensor("w0s", [8, 8, 128, 512], BF16)
    d_w2s = nc.dram_tensor("w2s", [2, 32, 128, 512], BF16)

    d_y = nc.dram_tensor("y", [T, C], F32, kind="ExternalOutput")
    d_dbg = {}
    if _DEBUG:
        d_dbg["dbg_w0s"] = nc.dram_tensor("dbg_w0s", [8, 8, 128, 512], BF16, kind="ExternalOutput")
        d_dbg["dbg_w2s"] = nc.dram_tensor("dbg_w2s", [2, 32, 128, 512], BF16, kind="ExternalOutput")
        d_dbg["dbg_a1"] = nc.dram_tensor("dbg_a1", [128, 32 * TT], BF16, kind="ExternalOutput")
        d_dbg["dbg_a2"] = nc.dram_tensor("dbg_a2", [128, 32 * TT], BF16, kind="ExternalOutput")
        d_dbg["dbg_xnfm"] = nc.dram_tensor("dbg_xnfm", [128, 8 * TT], BF16, kind="ExternalOutput")
        d_dbg["dbg_sreg"] = nc.dram_tensor("dbg_sreg", [128, 3], F32, kind="ExternalOutput")

    es = contextlib.ExitStack()
    sb = lambda n, s, d: es.enter_context(nc.sbuf_tensor(n, s, d))
    ps = lambda n, s, d: es.enter_context(nc.psum_tensor(n, s, d))

    H = {}
    H["ident"] = sb("sb_ident", [128, 128], BF16)
    H["triu"] = sb("sb_triu", [128, 128], BF16)
    H["ones_sq"] = sb("sb_ones_sq", [128, 128], BF16)
    H["ones_row"] = sb("sb_ones_row", [1, 128], BF16)
    H["ones_col"] = sb("sb_ones_col", [128, 1], BF16)
    H["ones_f32"] = sb("sb_ones_f32", [1, 128], F32)
    H["recip"] = sb("recip", [128, T // 128], F32)
    H["svals"] = sb("sb_svals", [1, 3], F32)
    H["sreg"] = sb("sreg", [128, 3], F32)
    H["xs"] = [sb(f"xs{i}", [128, C], F32) for i in range(4)]
    H["xn"] = [sb(f"xn{i}", [128, C], BF16) for i in range(NCH)]
    H["xnfm"] = sb("xnfm", [128, 8 * TT], BF16)
    H["a1"] = sb("a1", [128, 32 * TT], BF16)
    H["a2"] = sb("a2", [128, 32 * TT], BF16)
    H["acc"] = [sb(f"acc{i}", [128, C], F32) for i in range(NCH)]
    for n in ("st_var", "st_lnv", "st_rstd"):
        H[n] = sb(n, [128, NCH], F32)
    H["st12"] = sb("st12", [128, 12 * NCH], F32)
    H["st2"] = sb("st2", [128, 2 * NCH], F32)
    H["pref32"] = sb("pref32", [1, C], F32)
    H["prefbf"] = sb("prefbf", [1, C], BF16)
    H["w1b"] = [sb(f"w1b{p}", [128, 8 * 512], BF16) for p in range(4)]
    H["w2b"] = [sb(f"w2b{p}", [128, 4 * 512], BF16) for p in range(3)]
    H["ku8"] = sb("ku8", [128, 8192], U8)
    H["cv"] = [sb(f"cv{p}", [128, 4096], BF16) for p in range(2)]
    H["k1u8"] = [sb(f"k1u8_{p}", [128, 4096], U8) for p in range(4)]
    H["pb"] = [ps(f"pb{i}", [128, 512], F32) for i in range(8)]

    sems = {k: es.enter_context(nc.semaphore(k)) for k in SEMS}

    D = dict(x=d_x, k0=d_k0, k1=d_k1, k2=d_k2, sv=d_sv, ident=d_ident,
             triu=d_triu, ones_sq=d_ones_sq, ones_row=d_ones_row,
             ones_col=d_ones_col, ones_f32=d_ones_f32,
             w0s=d_w0s, w2s=d_w2s, y=d_y, dbg=d_dbg)

    with nc.Block() as block:
        for role in ROLES:
            def runner(eng, role=role):
                program(Ctx(role, eng, sems), H, D)
            getattr(block, role)(runner)

    es.close()
    return nc


def program(ctx, H, D):
    em = ctx.emit
    bank_ptr = [0]


    def alloc_banks(n):
        out = []
        for _ in range(n):
            out.append(bank_ptr[0] % 8)
            bank_ptr[0] += 1
        return out

    for name, dr, hb in [
        ("ident", D["ident"], H["ident"]), ("triu", D["triu"], H["triu"]),
        ("ones_sq", D["ones_sq"], H["ones_sq"]),
        ("ones_row", D["ones_row"], H["ones_row"]),
        ("ones_col", D["ones_col"], H["ones_col"]),
        ("ones_f32", D["ones_f32"], H["ones_f32"]),
        ("svals", D["sv"], H["svals"]),
    ]:
        em("sync", lambda e, dr=dr, hb=hb: e.dma_start(hb[:, :], dr[:, :]),
           "qs", 16, writes=[name])

    for c in range(NCH):
        em("sync", lambda e, c=c: e.dma_start(
            H["xs"][c][:, :], D["x"][c * 128:(c + 1) * 128, :]),
           "qs", 16, writes=[f"x{c}"])

    em("vector", lambda e: e.memset(H["pref32"][:, :], 0.0), "ve", 1, writes=["pref32"])
    em("vector", lambda e: e.memset(H["prefbf"][:, :], 0.0), "ve", 1, writes=["prefbf"])

    sb_bank = alloc_banks(1)[0]
    em("tensor", lambda e: e.matmul(
        H["pb"][sb_bank][:, 0:3], H["ones_f32"][:, :], H["svals"][:, :],
        start=True, stop=True),
       "pe", 1, reads=["ones_f32", "svals"], writes=[f"pb{sb_bank}"])
    em("vector", lambda e: e.tensor_scalar_mul(
        H["sreg"][:, :], H["pb"][sb_bank][:, 0:3], 2.0),
       "ve", 1, reads=[f"pb{sb_bank}"], writes=["sreg"])

    def load_ku8(owner, src, nparts):
        for part in range(nparts):
            em(owner, lambda e, src=src, part=part: e.dma_start(
                H["ku8"][:, :].rearrange("p (kc q) -> p kc q", q=512)[
                    :, part * 4:(part + 1) * 4, :],
                src[:, part * 4:(part + 1) * 4, :]),
               "qs" if owner == "sync" else "qg", 16, writes=[f"ku8r{part}"])

    def init_k0_load(g):
        sl = slice(g * 512, (g + 1) * 512)
        src = D["k0"][:, sl].rearrange("(kc p) q -> p kc q", p=128)
        if g % 2 == 0:
            for part in range(2):
                em("gpsimd", lambda e, src=src, part=part: e.dma_start(
                    H["ku8"][:, :].rearrange("p (kc q) -> p kc q", q=512)[
                        :, part * 8:(part + 1) * 8, :],
                    src[:, part * 8:(part + 1) * 8, :]),
                   "qg", 16, writes=[f"ku8r{2 * part}", f"ku8r{2 * part + 1}"])
        else:
            for part in range(2):
                buf = H["k1u8"][part]
                em("gpsimd", lambda e, src=src, part=part, buf=buf: e.dma_start(
                    buf[:, :].rearrange("p (kc q) -> p kc q", q=512),
                    src[:, part * 8:(part + 1) * 8, :]),
                   "qg", 16, writes=[f"k1u8_{part}h0", f"k1u8_{part}h1"])

    def init_k0_conv(g):
        p = g % 2
        if g % 2 == 0:
            em("vector", lambda e, p=p: e.tensor_tensor(
                H["cv"][p][:, :], H["ku8"][:, 0:4096],
                H["ku8"][:, 4096:8192], op=ALU.subtract),
               "ve", 1, reads=["ku8r0", "ku8r1", "ku8r2", "ku8r3"],
               writes=[f"cv{p}s{j}" for j in range(8)])
        else:
            em("vector", lambda e, p=p: e.tensor_tensor(
                H["cv"][p][:, :], H["k1u8"][0][:, :],
                H["k1u8"][1][:, :], op=ALU.subtract),
               "ve", 1,
               reads=["k1u8_0h0", "k1u8_0h1", "k1u8_1h0", "k1u8_1h1"],
               writes=[f"cv{p}s{j}" for j in range(8)])
        dst = D["w0s"][g].rearrange("kc p q -> p kc q")
        em("sync", lambda e, dst=dst, p=p: e.dma_start(
            dst, H["cv"][p][:, :].rearrange("p (kc q) -> p kc q", q=512)),
           "qs", 16, reads=[f"cv{p}s{j}" for j in range(8)],
           writes=[f"scr_w0_{g}"])

    def init_k0(g):
        init_k0_load(g)
        init_k0_conv(g)


    cvq = [0]

    def conv_quarter(q, dst_scr, scr_name):
        src_u8_slice = H["ku8"][:, q * 4096:(q + 1) * 4096]
        ku8_regions = [f"ku8r{2 * q}", f"ku8r{2 * q + 1}"]
        p = cvq[0] % 2
        cvq[0] += 1
        cv_slices = [f"cv{p}s{j}" for j in range(8)]
        em("vector", lambda e, p=p, src_u8_slice=src_u8_slice: e.tensor_scalar(
            H["cv"][p][:, :], src_u8_slice, 2.0, -1.0,
            op0=ALU.mult, op1=ALU.add),
            "ve", 1, reads=ku8_regions, writes=cv_slices)
        em("gpsimd", lambda e, p=p, dst_scr=dst_scr: e.dma_start(
            dst_scr, H["cv"][p][:, :].rearrange("p (kc q) -> p kc q", q=512)),
           "qg", 16, reads=cv_slices, writes=[scr_name])

    def init_k2_unit(u):
        h, rh = divmod(u, 2)
        sl = slice(h * 512, (h + 1) * 512)
        src = D["k2"][rh * 2048:(rh + 1) * 2048, sl].rearrange(
            "(kc p) q -> p kc q", p=128)
        load_ku8("gpsimd", src, 4)
        for qq in range(2):
            q = rh * 2 + qq
            conv_quarter(qq, D["w2s"][h, q * 8:(q + 1) * 8].rearrange(
                "kc p q -> p kc q"), f"scr_w2_{h}_{q}")

    xn_v = H["xnfm"][:, :].rearrange("p (kc t) -> p kc t", t=TT)

    def load_x(t):
        for c in range(NCH):
            row0 = (t * NCH + c) * 128
            em("sync", lambda e, c=c, row0=row0: e.dma_start(
                H["xs"][c][:, :], D["x"][row0:row0 + 128, :]),
               "qs", 16, reads=[], writes=[f"x{c}"])

    def ln_tile(chunks=tuple(range(NCH))):
        for c in chunks:
            em("vector", lambda e, c=c: e.bn_stats(
                H["st12"][:, 12 * c:12 * c + 6], H["xs"][c][:, 0:512]),
               "ve", 1, reads=[f"x{c}"], writes=[f"st12a{c}"])
            em("vector", lambda e, c=c: e.bn_stats(
                H["st12"][:, 12 * c + 6:12 * c + 12], H["xs"][c][:, 512:1024]),
               "ve", 1, reads=[f"x{c}"], writes=[f"st12b{c}"])
        for c in chunks:
            em("vector", lambda e, c=c: e.bn_aggr(
                H["st2"][:, 2 * c:2 * c + 2], H["st12"][:, 12 * c:12 * c + 12]),
               "ve", 1, reads=[f"st12a{c}", f"st12b{c}"], writes=[f"st2{c}"])
            em("vector", lambda e, c=c: e.tensor_scalar_add(
                H["st_var"][:, c:c + 1], H["st2"][:, 2 * c + 1:2 * c + 2], EPS),
               "ve", 1, reads=[f"st2{c}"], writes=[f"st_var{c}"])
        for c in chunks:
            em("scalar", lambda e, c=c: e.activation(
                H["st_lnv"][:, c:c + 1], H["st_var"][:, c:c + 1], AF.Ln),
               "ae", 1, reads=[f"st_var{c}"], writes=[f"st_lnv{c}"])
            em("scalar", lambda e, c=c: e.activation(
                H["st_rstd"][:, c:c + 1], H["st_lnv"][:, c:c + 1], AF.Exp,
                scale=-0.5),
               "ae", 1, reads=[f"st_lnv{c}"], writes=[f"st_rstd{c}"])
        for c in chunks:
            em("vector", lambda e, c=c: e.tensor_scalar(
                H["xn"][c][:, :], H["xs"][c][:, :], H["st2"][:, 2 * c:2 * c + 1],
                H["st_rstd"][:, c:c + 1], op0=ALU.subtract, op1=ALU.mult),
               "ve", 1, reads=[f"x{c}", f"st2{c}", f"st_rstd{c}"],
               writes=[f"xn{c}"])

    init_k0_load(0)
    init_k0_load(1)
    em("gpsimd", lambda e: e.iota(
        H["recip"][:, :], [[128, T // 128]], base=1, channel_multiplier=1,
        allow_small_or_imprecise_dtypes=True),
       "gx", 1, writes=["recip_iota"])
    ln_tile()
    init_k0_conv(0)
    init_k0_conv(1)
    em("vector", lambda e: e.reciprocal(H["recip"][:, :], H["recip"][:, :]),
       "ve", 1, reads=["recip_iota"], writes=["recip"])

    k1cv = [0]

    def emit_w1_piece(t, mcg, q):
        pidx = t * 32 + mcg * 4 + q
        pp = pidx % 4
        up = pidx % 4
        rsl = slice(q * 1024, (q + 1) * 1024)
        csl = slice(mcg * 512, (mcg + 1) * 512)
        ksrc = D["k1"][rsl, csl].rearrange("(kc p) q -> p kc q", p=128)
        for half in range(2):
            em("sync", lambda e, ksrc=ksrc, up=up, half=half: e.dma_start(
                H["k1u8"][up][:, :].rearrange("p (kc q) -> p kc q", q=512)[
                    :, half * 4:(half + 1) * 4, :],
                ksrc[:, half * 4:(half + 1) * 4, :]),
               "qs", 16, writes=[f"k1u8_{up}h{half}"])
        em("scalar", lambda e, pp=pp, up=up: e.activation(
            H["w1b"][pp][:, :], H["k1u8"][up][:, :], AF.Copy,
            bias=-1.0, scale=2.0),
           "ae", 1, reads=[f"k1u8_{up}h0", f"k1u8_{up}h1"],
           writes=[f"w1b{pp}hA"])


    for t in range(NTILE):
        for c in range(NCH):
            g = t * NCH + c
            tA, tB = alloc_banks(2)
            em("tensor", lambda e, c=c, tA=tA, tB=tB: [
                e.matmul(H["pb"][(tA, tB)[kc // 4]][:, (kc % 4) * 128:(kc % 4 + 1) * 128],
                         H["xn"][c][:, kc * 128:(kc + 1) * 128],
                         H["ident"][:, :], start=True, stop=True)
                for kc in range(8)][-1],
               "pe", 1, reads=[f"xn{c}", "ident"], writes=[f"pb{tA}", f"pb{tB}"])
            em("scalar", lambda e, c=c, tA=tA: e.copy(
                xn_v[:, 0:4, c * 128:(c + 1) * 128],
                H["pb"][tA][:, :].rearrange("p (kc j) -> p kc j", j=128)),
               "ae", 1, reads=[f"pb{tA}"], writes=[f"xnfm{c}_lo"])
            em("vector", lambda e, c=c, tB=tB: e.tensor_copy(
                xn_v[:, 4:8, c * 128:(c + 1) * 128],
                H["pb"][tB][:, :].rearrange("p (kc j) -> p kc j", j=128)),
               "ve", 1, reads=[f"pb{tB}"], writes=[f"xnfm{c}"])
            bA, bB = alloc_banks(2)
            banks = (bA, bB)

            def attn_mms(e, c=c, banks=banks):
                last = None
                for h in range(2):
                    hs = slice(h * 512, (h + 1) * 512)
                    pb = H["pb"][banks[h]][:, :]
                    last = e.matmul(pb, H["ones_row"][:, :],
                                    H["prefbf"][:, hs], start=True, stop=False)
                    for sc in range(c):
                        last = e.matmul(pb, H["ones_sq"][:, :],
                                        H["xn"][sc][:, hs], start=False, stop=False)
                    last = e.matmul(pb, H["triu"][:, :],
                                    H["xn"][c][:, hs], start=False, stop=True)
                return last
            em("tensor", attn_mms, "pe", 1,
               reads=["ones_row", "ones_sq", "triu", "prefbf"]
                     + [f"xn{sc}" for sc in range(c + 1)],
               writes=[f"pb{bA}", f"pb{bB}"])
            cA, cB = alloc_banks(2)

            def colsum_mms(e, c=c, cA=cA, cB=cB):
                e.matmul(H["pb"][cA][0:1, :], H["ones_col"][:, :],
                         H["xn"][c][:, 0:512], start=True, stop=True)
                return e.matmul(H["pb"][cB][0:1, :], H["ones_col"][:, :],
                                H["xn"][c][:, 512:1024], start=True, stop=True)
            em("tensor", colsum_mms, "pe", 1,
               reads=["ones_col", f"xn{c}"], writes=[f"pb{cA}", f"pb{cB}"])
            for h, b in ((0, bA), (1, bB)):
                hs = slice(h * 512, (h + 1) * 512)
                em("vector", lambda e, c=c, b=b, hs=hs, g=g: e.scalar_tensor_tensor(
                    H["acc"][c][:, hs], H["pb"][b][:, :],
                    H["recip"][:, g:g + 1], H["xs"][c][:, hs],
                    op0=ALU.mult, op1=ALU.add),
                   "ve", 1, reads=[f"pb{b}", "recip", f"x{c}"],
                   writes=[f"acc{c}h{h}"])
            for h, b in ((0, cA), (1, cB)):
                hs = slice(h * 512, (h + 1) * 512)
                em("vector", lambda e, b=b, hs=hs: e.tensor_tensor(
                    H["pref32"][0:1, hs], H["pref32"][0:1, hs],
                    H["pb"][b][0:1, :], op=ALU.add),
                   "ve", 1, reads=[f"pb{b}", "pref32"], writes=["pref32"])
        em("vector", lambda e: e.tensor_copy(H["prefbf"][0:1, :], H["pref32"][0:1, :]),
           "ve", 1, reads=["pref32"], writes=["prefbf"])

        if t + 1 < NTILE:
            load_x(t + 1)

        preconverted = {(0, 0), (0, 1)}
        for mcg_, q_ in sorted(preconverted):
            emit_w1_piece(t, mcg_, q_)

        for mcg in range(8):
            pp = mcg % 2
            if t == 0:
                if mcg >= 2:
                    init_k0(mcg)
            elif mcg >= 2:
                em("gpsimd", lambda e, mcg=mcg, pp=pp: e.dma_start(
                    H["cv"][pp][:, :].rearrange("p (kc q) -> p kc q", q=512),
                    D["w0s"][mcg].rearrange("kc p q -> p kc q")),
                   "qg", 16, reads=[f"scr_w0_{mcg}"],
                   writes=[f"cv{pp}s{j}" for j in range(8)])
            wsrc = H["cv"][pp]
            wreads = [f"cv{pp}s{j}" for j in range(8)]
            banks = alloc_banks(4)
            for ml in range(4):
                b = banks[ml]

                def mm0_bank(e, wsrc=wsrc, b=b, ml=ml):
                    last = None
                    for kc in range(8):
                        last = e.matmul(
                            H["pb"][b][:, :],
                            wsrc[:, kc * 512 + ml * 128: kc * 512 + (ml + 1) * 128],
                            xn_v[:, kc, :],
                            start=(kc == 0), stop=(kc == 7))
                    return last
                em("tensor", mm0_bank, "pe", 1,
                   reads=wreads + [f"xnfm{c}" for c in range(NCH)]
                         + [f"xnfm{c}_lo" for c in range(NCH)],
                   writes=[f"pb{b}"])
            for ml in range(4):
                mc = mcg * 4 + ml
                b = banks[ml]
                if t == 0:
                    em("scalar", lambda e, mc=mc, b=b: e.activation(
                        H["a1"][:, mc * TT:(mc + 1) * TT], H["pb"][b][:, :],
                        AF.Relu, scale=H["sreg"][:, 0:1]),
                       "ae", 1, reads=[f"pb{b}", "sreg"], writes=[f"a1_{mc}"])
                else:
                    em("vector", lambda e, mc=mc, b=b: e.tensor_scalar(
                        H["a1"][:, mc * TT:(mc + 1) * TT], H["pb"][b][:, :],
                        H["sreg"][:, 0:1], 0.0, op0=ALU.mult, op1=ALU.max),
                       "ve", 1, reads=[f"pb{b}", "sreg"], writes=[f"a1_{mc}"])


        for mcg in range(8):
            if t == 0 and mcg % 2 == 1:
                init_k2_unit(mcg // 2)
            banks = alloc_banks(4)
            for q in range(4):
                if (mcg, q) not in preconverted:
                    emit_w1_piece(t, mcg, q)
                pp = (t * 32 + mcg * 4 + q) % 4

                for ml in range(4):
                    b = banks[ml]

                    def mm1_bank(e, pp=pp, q=q, b=b, ml=ml):
                        last = None
                        for kcl in range(8):
                            kc = q * 8 + kcl
                            last = e.matmul(
                                H["pb"][b][:, :],
                                H["w1b"][pp][:, kcl * 512 + ml * 128: kcl * 512 + (ml + 1) * 128],
                                H["a1"][:, kc * TT:(kc + 1) * TT],
                                start=(kc == 0), stop=(kc == 31))
                        return last
                    em("tensor", mm1_bank, "pe", 1,
                       reads=[f"w1b{pp}hA"]
                             + [f"a1_{q * 8 + kcl}" for kcl in range(8)],
                       writes=[f"pb{b}"])
            for ml in range(4):
                mc = mcg * 4 + ml
                b = banks[ml]
                em("vector", lambda e, mc=mc, b=b: e.tensor_scalar(
                    H["a2"][:, mc * TT:(mc + 1) * TT], H["pb"][b][:, :],
                    H["sreg"][:, 1:2], 0.0, op0=ALU.mult, op1=ALU.max),
                   "ve", 1, reads=[f"pb{b}", "sreg"], writes=[f"a2_{mc}"])

        if t + 1 < NTILE:
            for mcg_ in range(2):
                em("gpsimd", lambda e, mcg_=mcg_: e.dma_start(
                    H["cv"][mcg_][:, :].rearrange("p (kc q) -> p kc q", q=512),
                    D["w0s"][mcg_].rearrange("kc p q -> p kc q")),
                   "qg", 16, reads=[f"scr_w0_{mcg_}"],
                   writes=[f"cv{mcg_}s{j}" for j in range(8)])

        for h in range(2):
            if h == 1 and t + 1 < NTILE:
                ln_tile()
            banks = alloc_banks(4)
            for piece in range(8):
                pp = (h * 8 + piece) % 3
                em("gpsimd", lambda e, h=h, piece=piece, pp=pp: e.dma_start(
                    H["w2b"][pp][:, :].rearrange("p (kc q) -> p kc q", q=512),
                    D["w2s"][h, piece * 4:(piece + 1) * 4].rearrange("kc p q -> p kc q")),
                   "qg", 16, reads=[f"scr_w2_{h}_{piece // 2}"],
                   writes=[f"w2b{pp}"])

                for tc in range(4):
                    b = banks[tc]

                    def mm2_bank(e, piece=piece, pp=pp, b=b, tc=tc):
                        last = None
                        for kcl in range(4):
                            kc = piece * 4 + kcl
                            last = e.matmul(
                                H["pb"][b][:, :],
                                H["a2"][:, kc * TT + tc * 128: kc * TT + (tc + 1) * 128],
                                H["w2b"][pp][:, kcl * 512:(kcl + 1) * 512],
                                start=(kc == 0), stop=(kc == 31))
                        return last
                    em("tensor", mm2_bank, "pe", 1,
                       reads=[f"w2b{pp}"] + [f"a2_{piece * 4 + kcl}" for kcl in range(4)],
                       writes=[f"pb{b}"])
            hs = slice(h * 512, (h + 1) * 512)
            for tc in range(4):
                b = banks[tc]
                em("vector", lambda e, tc=tc, b=b, hs=hs: e.scalar_tensor_tensor(
                    H["acc"][tc][:, hs], H["pb"][b][:, :],
                    H["sreg"][:, 2:3], H["acc"][tc][:, hs],
                    op0=ALU.mult, op1=ALU.add),
                   "ve", 1, reads=[f"pb{b}", "sreg", f"acc{tc}h{h}"],
                   writes=[f"acc{tc}h{h}", f"out{tc}h{h}"])
        for c in range(NCH):
            row0 = (t * NCH + c) * 128
            em("gpsimd", lambda e, c=c, row0=row0: e.dma_start(
                D["y"][row0:row0 + 128, :], H["acc"][c][:, :]),
               "qg", 16, reads=[f"out{c}h0", f"out{c}h1",
                                f"acc{c}h0", f"acc{c}h1"],
               writes=[f"ydone{t}_{c}"])

    if _DEBUG:
        for nm, src, deps in [
            ("dbg_w0s", D["w0s"], [f"scr_w0_{g}" for g in range(8)]),
            ("dbg_w2s", D["w2s"], [f"scr_w2_{h}_{q}" for h in range(2) for q in range(4)]),
        ]:
            em("gpsimd", lambda e, nm=nm, src=src: e.dma_start(
                D["dbg"][nm][:, :, :, :], src[:, :, :, :]),
               "qg", 16, reads=deps)
        em("gpsimd", lambda e: e.dma_start(D["dbg"]["dbg_a1"][:, :], H["a1"][:, :]),
           "qg", 16, reads=[f"a1_{mc}" for mc in range(32)])
        em("gpsimd", lambda e: e.dma_start(D["dbg"]["dbg_a2"][:, :], H["a2"][:, :]),
           "qg", 16, reads=[f"a2_{mc}" for mc in range(32)])
        em("gpsimd", lambda e: e.dma_start(D["dbg"]["dbg_xnfm"][:, :], H["xnfm"][:, :]),
           "qg", 16, reads=[f"xnfm{c}" for c in range(NCH)])
        em("gpsimd", lambda e: e.dma_start(D["dbg"]["dbg_sreg"][:, :], H["sreg"][:, :]),
           "qg", 16, reads=["sreg"])

    if ctx.role == "gpsimd":
        for i in range(NPOOL):
            if ctx.cnt[f"qg{i}"] > 0:
                ctx.eng.wait_ge(ctx.sems[f"qg{i}"], ctx.cnt[f"qg{i}"])


_consts_cache = {}


def _make_consts():
    if _consts_cache:
        return _consts_cache
    _consts_cache.update(
        ident=np.eye(128, dtype=ml_dtypes.bfloat16),
        triu=np.triu(np.ones((128, 128))).astype(ml_dtypes.bfloat16),
        ones_sq=np.ones((128, 128), dtype=ml_dtypes.bfloat16),
        ones_row=np.ones((1, 128), dtype=ml_dtypes.bfloat16),
        ones_col=np.ones((128, 1), dtype=ml_dtypes.bfloat16),
        ones_f32=np.ones((1, 128), dtype=np.float32),
    )
    return _consts_cache


def kernel(L0_attn_x, L0_ffn_k0, L0_ffn_k1, L0_ffn_k2, L0_ffn_s0, L0_ffn_s1,
           L0_ffn_s2):
    x = np.ascontiguousarray(np.asarray(L0_attn_x, dtype=np.float32))
    k0 = np.ascontiguousarray(np.asarray(L0_ffn_k0)).view(np.uint8)
    k1 = np.ascontiguousarray(np.asarray(L0_ffn_k1)).view(np.uint8)
    k2 = np.ascontiguousarray(np.asarray(L0_ffn_k2)).view(np.uint8)
    svals = np.ascontiguousarray(
        np.stack([np.asarray(L0_ffn_s0), np.asarray(L0_ffn_s1),
                  np.asarray(L0_ffn_s2)]).astype(np.float32).reshape(1, 3))
    cn = _make_consts()

    nc = build_nc()
    in_maps = []
    for b in range(8):
        in_maps.append(dict(
            x=np.ascontiguousarray(x[b]), k0=k0, k1=k1, k2=k2, svals=svals,
            ident=cn["ident"], triu=cn["triu"], ones_sq=cn["ones_sq"],
            ones_row=cn["ones_row"], ones_col=cn["ones_col"],
            ones_f32=cn["ones_f32"]))
        if _DEBUG_W:
            in_maps[-1].update(_DEBUG_W)
    res = run_bass_kernel_spmd(nc, in_maps, core_ids=list(range(8)))
    global _LAST
    _LAST = res
    return np.stack([np.asarray(res.results[b]["y"]) for b in range(8)], axis=0)


_LAST = None


# version 42 (final)
# speedup vs baseline: 1.0228x; 1.0228x over previous
import contextlib
import numpy as np
import ml_dtypes
import concourse.bass as bass
import concourse.mybir as mybir
from concourse.bass_utils import run_bass_kernel_spmd

F32 = mybir.dt.float32
BF16 = mybir.dt.bfloat16
U8 = mybir.dt.uint8
AF = mybir.ActivationFunctionType
ALU = mybir.AluOpType
AX = mybir.AxisListType

T = 2048
C = 1024
TT = 512
NTILE = T // TT
NCH = TT // 128
F1 = 4096
F2 = 4096
EPS = 1e-5

ROLES = ("sync", "tensor", "vector", "scalar", "gpsimd")
NPOOL = 12
SEMS = (("pe", "ve", "ae", "gx")
        + tuple(f"qs{i}" for i in range(NPOOL))
        + tuple(f"qg{i}" for i in range(NPOOL)))
_DEBUG = False
_DEBUG_W = None


class Ctx:

    def __init__(self, role, eng, sems):
        self.role = role
        self.eng = eng
        self.sems = sems
        self.cnt = {k: 0 for k in SEMS}
        self.res = {}
        self.waited = {k: 0 for k in SEMS}
        self.dma_idx = {"qs": 0, "qg": 0}

    def emit(self, owner, build_fn, inc_sem, inc_n=1, reads=(), writes=()):
        deps = []
        if inc_sem in ("qs", "qg"):
            q = inc_sem
            inc_sem = f"{q}{self.dma_idx[q] % NPOOL}"
            self.dma_idx[q] += 1
            if self.cnt[inc_sem] > 0:
                deps.append((inc_sem, self.cnt[inc_sem]))
        for r in reads:
            e = self.res.get(r)
            if e and e["w"]:
                deps.append(e["w"])
        for w in writes:
            e = self.res.get(w)
            if e:
                deps.extend(e["r"])
                if e["w"]:
                    deps.append(e["w"])
        if self.role == owner:
            merged = {}
            for s, v in deps:
                if owner == "tensor" and s == "pe":
                    continue
                merged[s] = max(v, merged.get(s, 0))
            for s, v in sorted(merged.items()):
                if v > self.waited[s]:
                    self.eng.wait_ge(self.sems[s], v)
                    self.waited[s] = v
            inst = build_fn(self.eng)
            inst.then_inc(self.sems[inc_sem], inc_n)
        self.cnt[inc_sem] += inc_n
        tok = (inc_sem, self.cnt[inc_sem])
        for r in reads:
            self.res.setdefault(r, {"w": None, "r": []})["r"].append(tok)
        for w in writes:
            self.res[w] = {"w": tok, "r": []}
        return tok


def build_nc():
    nc = bass.Bass("TRN2", target_bir_lowering=False)

    d_x = nc.dram_tensor("x", [T, C], F32, kind="ExternalInput")
    d_k0 = nc.dram_tensor("k0", [2 * C, F1], U8, kind="ExternalInput")
    d_k1 = nc.dram_tensor("k1", [F1, F2], U8, kind="ExternalInput")
    d_k2 = nc.dram_tensor("k2", [F2, C], U8, kind="ExternalInput")
    d_sv = nc.dram_tensor("svals", [1, 3], F32, kind="ExternalInput")
    d_ident = nc.dram_tensor("ident", [128, 128], BF16, kind="ExternalInput")
    d_triu = nc.dram_tensor("triu", [128, 128], BF16, kind="ExternalInput")
    d_ones_sq = nc.dram_tensor("ones_sq", [128, 128], BF16, kind="ExternalInput")
    d_ones_row = nc.dram_tensor("ones_row", [1, 128], BF16, kind="ExternalInput")
    d_ones_col = nc.dram_tensor("ones_col", [128, 1], BF16, kind="ExternalInput")
    d_ones_f32 = nc.dram_tensor("ones_f32", [1, 128], F32, kind="ExternalInput")

    d_w0s = nc.dram_tensor("w0s", [8, 8, 128, 512], BF16)
    d_w2s = nc.dram_tensor("w2s", [2, 32, 128, 512], BF16)

    d_y = nc.dram_tensor("y", [T, C], F32, kind="ExternalOutput")
    d_dbg = {}
    if _DEBUG:
        d_dbg["dbg_w0s"] = nc.dram_tensor("dbg_w0s", [8, 8, 128, 512], BF16, kind="ExternalOutput")
        d_dbg["dbg_w2s"] = nc.dram_tensor("dbg_w2s", [2, 32, 128, 512], BF16, kind="ExternalOutput")
        d_dbg["dbg_a1"] = nc.dram_tensor("dbg_a1", [128, 32 * TT], BF16, kind="ExternalOutput")
        d_dbg["dbg_a2"] = nc.dram_tensor("dbg_a2", [128, 32 * TT], BF16, kind="ExternalOutput")
        d_dbg["dbg_xnfm"] = nc.dram_tensor("dbg_xnfm", [128, 8 * TT], BF16, kind="ExternalOutput")
        d_dbg["dbg_sreg"] = nc.dram_tensor("dbg_sreg", [128, 3], F32, kind="ExternalOutput")

    es = contextlib.ExitStack()
    sb = lambda n, s, d: es.enter_context(nc.sbuf_tensor(n, s, d))
    ps = lambda n, s, d: es.enter_context(nc.psum_tensor(n, s, d))

    H = {}
    H["ident"] = sb("sb_ident", [128, 128], BF16)
    H["triu"] = sb("sb_triu", [128, 128], BF16)
    H["ones_sq"] = sb("sb_ones_sq", [128, 128], BF16)
    H["ones_row"] = sb("sb_ones_row", [1, 128], BF16)
    H["ones_col"] = sb("sb_ones_col", [128, 1], BF16)
    H["ones_f32"] = sb("sb_ones_f32", [1, 128], F32)
    H["recip"] = sb("recip", [128, T // 128], F32)
    H["svals"] = sb("sb_svals", [1, 3], F32)
    H["sreg"] = sb("sreg", [128, 3], F32)
    H["xs"] = [sb(f"xs{i}", [128, C], F32) for i in range(4)]
    H["xn"] = [sb(f"xn{i}", [128, C], BF16) for i in range(NCH)]
    H["xnfm"] = sb("xnfm", [128, 8 * TT], BF16)
    H["a1"] = sb("a1", [128, 32 * TT], BF16)
    H["a2"] = sb("a2", [128, 32 * TT], BF16)
    H["acc"] = [sb(f"acc{i}", [128, C], F32) for i in range(NCH)]
    for n in ("st_var", "st_lnv", "st_rstd"):
        H[n] = sb(n, [128, NCH], F32)
    H["st12"] = sb("st12", [128, 12 * NCH], F32)
    H["st2"] = sb("st2", [128, 2 * NCH], F32)
    H["pref32"] = sb("pref32", [1, C], F32)
    H["prefbf"] = sb("prefbf", [1, C], BF16)
    H["w1b"] = [sb(f"w1b{p}", [128, 8 * 512], BF16) for p in range(4)]
    H["w2b"] = [sb(f"w2b{p}", [128, 4 * 512], BF16) for p in range(3)]
    H["ku8"] = sb("ku8", [128, 8192], U8)
    H["cv"] = [sb(f"cv{p}", [128, 4096], BF16) for p in range(2)]
    H["k1u8"] = [sb(f"k1u8_{p}", [128, 4096], U8) for p in range(4)]
    H["pb"] = [ps(f"pb{i}", [128, 512], F32) for i in range(8)]

    sems = {k: es.enter_context(nc.semaphore(k)) for k in SEMS}

    D = dict(x=d_x, k0=d_k0, k1=d_k1, k2=d_k2, sv=d_sv, ident=d_ident,
             triu=d_triu, ones_sq=d_ones_sq, ones_row=d_ones_row,
             ones_col=d_ones_col, ones_f32=d_ones_f32,
             w0s=d_w0s, w2s=d_w2s, y=d_y, dbg=d_dbg)

    with nc.Block() as block:
        for role in ROLES:
            def runner(eng, role=role):
                program(Ctx(role, eng, sems), H, D)
            getattr(block, role)(runner)

    es.close()
    return nc


def program(ctx, H, D):
    em = ctx.emit
    bank_ptr = [0]


    def alloc_banks(n):
        out = []
        for _ in range(n):
            out.append(bank_ptr[0] % 8)
            bank_ptr[0] += 1
        return out

    for name, dr, hb in [
        ("ident", D["ident"], H["ident"]), ("triu", D["triu"], H["triu"]),
        ("ones_sq", D["ones_sq"], H["ones_sq"]),
        ("ones_row", D["ones_row"], H["ones_row"]),
        ("ones_col", D["ones_col"], H["ones_col"]),
        ("ones_f32", D["ones_f32"], H["ones_f32"]),
        ("svals", D["sv"], H["svals"]),
    ]:
        em("sync", lambda e, dr=dr, hb=hb: e.dma_start(hb[:, :], dr[:, :]),
           "qs", 16, writes=[name])

    for c in range(NCH):
        em("sync", lambda e, c=c: e.dma_start(
            H["xs"][c][:, :], D["x"][c * 128:(c + 1) * 128, :]),
           "qs", 16, writes=[f"x{c}"])

    em("vector", lambda e: e.memset(H["pref32"][:, :], 0.0), "ve", 1, writes=["pref32"])
    em("vector", lambda e: e.memset(H["prefbf"][:, :], 0.0), "ve", 1, writes=["prefbf"])

    sb_bank = alloc_banks(1)[0]
    em("tensor", lambda e: e.matmul(
        H["pb"][sb_bank][:, 0:3], H["ones_f32"][:, :], H["svals"][:, :],
        start=True, stop=True),
       "pe", 1, reads=["ones_f32", "svals"], writes=[f"pb{sb_bank}"])
    em("vector", lambda e: e.tensor_scalar_mul(
        H["sreg"][:, :], H["pb"][sb_bank][:, 0:3], 2.0),
       "ve", 1, reads=[f"pb{sb_bank}"], writes=["sreg"])

    def load_ku8(owner, src, nparts):
        for part in range(nparts):
            em(owner, lambda e, src=src, part=part: e.dma_start(
                H["ku8"][:, :].rearrange("p (kc q) -> p kc q", q=512)[
                    :, part * 4:(part + 1) * 4, :],
                src[:, part * 4:(part + 1) * 4, :]),
               "qs" if owner == "sync" else "qg", 16, writes=[f"ku8r{part}"])

    def init_k0_load(g):
        sl = slice(g * 512, (g + 1) * 512)
        src = D["k0"][:, sl].rearrange("(kc p) q -> p kc q", p=128)
        if g % 2 == 0:
            for part in range(2):
                em("gpsimd", lambda e, src=src, part=part: e.dma_start(
                    H["ku8"][:, :].rearrange("p (kc q) -> p kc q", q=512)[
                        :, part * 8:(part + 1) * 8, :],
                    src[:, part * 8:(part + 1) * 8, :]),
                   "qg", 16, writes=[f"ku8r{2 * part}", f"ku8r{2 * part + 1}"])
        else:
            for part in range(2):
                buf = H["k1u8"][part]
                em("gpsimd", lambda e, src=src, part=part, buf=buf: e.dma_start(
                    buf[:, :].rearrange("p (kc q) -> p kc q", q=512),
                    src[:, part * 8:(part + 1) * 8, :]),
                   "qg", 16, writes=[f"k1u8_{part}h0", f"k1u8_{part}h1"])

    def init_k0_conv(g):
        p = g % 2
        if g % 2 == 0:
            em("vector", lambda e, p=p: e.tensor_tensor(
                H["cv"][p][:, :], H["ku8"][:, 0:4096],
                H["ku8"][:, 4096:8192], op=ALU.subtract),
               "ve", 1, reads=["ku8r0", "ku8r1", "ku8r2", "ku8r3"],
               writes=[f"cv{p}s{j}" for j in range(8)])
        else:
            em("vector", lambda e, p=p: e.tensor_tensor(
                H["cv"][p][:, :], H["k1u8"][0][:, :],
                H["k1u8"][1][:, :], op=ALU.subtract),
               "ve", 1,
               reads=["k1u8_0h0", "k1u8_0h1", "k1u8_1h0", "k1u8_1h1"],
               writes=[f"cv{p}s{j}" for j in range(8)])
        dst = D["w0s"][g].rearrange("kc p q -> p kc q")
        em("sync", lambda e, dst=dst, p=p: e.dma_start(
            dst, H["cv"][p][:, :].rearrange("p (kc q) -> p kc q", q=512)),
           "qs", 16, reads=[f"cv{p}s{j}" for j in range(8)],
           writes=[f"scr_w0_{g}"])

    def init_k0(g):
        init_k0_load(g)
        init_k0_conv(g)


    cvq = [0]

    def conv_quarter(q, dst_scr, scr_name):
        src_u8_slice = H["ku8"][:, q * 4096:(q + 1) * 4096]
        ku8_regions = [f"ku8r{2 * q}", f"ku8r{2 * q + 1}"]
        p = cvq[0] % 2
        cvq[0] += 1
        cv_slices = [f"cv{p}s{j}" for j in range(8)]
        em("vector", lambda e, p=p, src_u8_slice=src_u8_slice: e.tensor_scalar(
            H["cv"][p][:, :], src_u8_slice, 2.0, -1.0,
            op0=ALU.mult, op1=ALU.add),
            "ve", 1, reads=ku8_regions, writes=cv_slices)
        em("gpsimd", lambda e, p=p, dst_scr=dst_scr: e.dma_start(
            dst_scr, H["cv"][p][:, :].rearrange("p (kc q) -> p kc q", q=512)),
           "qg", 16, reads=cv_slices, writes=[scr_name])

    def init_k2_unit(u):
        h, rh = divmod(u, 2)
        sl = slice(h * 512, (h + 1) * 512)
        src = D["k2"][rh * 2048:(rh + 1) * 2048, sl].rearrange(
            "(kc p) q -> p kc q", p=128)
        load_ku8("gpsimd", src, 4)
        for qq in range(2):
            q = rh * 2 + qq
            conv_quarter(qq, D["w2s"][h, q * 8:(q + 1) * 8].rearrange(
                "kc p q -> p kc q"), f"scr_w2_{h}_{q}")

    xn_v = H["xnfm"][:, :].rearrange("p (kc t) -> p kc t", t=TT)

    def load_x(t):
        for c in range(NCH):
            row0 = (t * NCH + c) * 128
            em("sync", lambda e, c=c, row0=row0: e.dma_start(
                H["xs"][c][:, :], D["x"][row0:row0 + 128, :]),
               "qs", 16, reads=[], writes=[f"x{c}"])

    def ln_tile(chunks=tuple(range(NCH))):
        for c in chunks:
            em("vector", lambda e, c=c: e.bn_stats(
                H["st12"][:, 12 * c:12 * c + 6], H["xs"][c][:, 0:512]),
               "ve", 1, reads=[f"x{c}"], writes=[f"st12a{c}"])
            em("vector", lambda e, c=c: e.bn_stats(
                H["st12"][:, 12 * c + 6:12 * c + 12], H["xs"][c][:, 512:1024]),
               "ve", 1, reads=[f"x{c}"], writes=[f"st12b{c}"])
        for c in chunks:
            em("vector", lambda e, c=c: e.bn_aggr(
                H["st2"][:, 2 * c:2 * c + 2], H["st12"][:, 12 * c:12 * c + 12]),
               "ve", 1, reads=[f"st12a{c}", f"st12b{c}"], writes=[f"st2{c}"])
            em("vector", lambda e, c=c: e.tensor_scalar_add(
                H["st_var"][:, c:c + 1], H["st2"][:, 2 * c + 1:2 * c + 2], EPS),
               "ve", 1, reads=[f"st2{c}"], writes=[f"st_var{c}"])
        for c in chunks:
            em("scalar", lambda e, c=c: e.activation(
                H["st_lnv"][:, c:c + 1], H["st_var"][:, c:c + 1], AF.Ln),
               "ae", 1, reads=[f"st_var{c}"], writes=[f"st_lnv{c}"])
            em("scalar", lambda e, c=c: e.activation(
                H["st_rstd"][:, c:c + 1], H["st_lnv"][:, c:c + 1], AF.Exp,
                scale=-0.5),
               "ae", 1, reads=[f"st_lnv{c}"], writes=[f"st_rstd{c}"])
        for c in chunks:
            em("vector", lambda e, c=c: e.tensor_scalar(
                H["xn"][c][:, :], H["xs"][c][:, :], H["st2"][:, 2 * c:2 * c + 1],
                H["st_rstd"][:, c:c + 1], op0=ALU.subtract, op1=ALU.mult),
               "ve", 1, reads=[f"x{c}", f"st2{c}", f"st_rstd{c}"],
               writes=[f"xn{c}"])

    init_k0_load(0)
    init_k0_load(1)
    em("gpsimd", lambda e: e.iota(
        H["recip"][:, :], [[128, T // 128]], base=1, channel_multiplier=1,
        allow_small_or_imprecise_dtypes=True),
       "gx", 1, writes=["recip_iota"])
    ln_tile()
    init_k0_conv(0)
    init_k0_conv(1)
    em("vector", lambda e: e.reciprocal(H["recip"][:, :], H["recip"][:, :]),
       "ve", 1, reads=["recip_iota"], writes=["recip"])

    k1cv = [0]

    def emit_w1_piece(t, mcg, q):
        pidx = t * 32 + mcg * 4 + q
        pp = pidx % 4
        up = pidx % 4
        rsl = slice(q * 1024, (q + 1) * 1024)
        csl = slice(mcg * 512, (mcg + 1) * 512)
        ksrc = D["k1"][rsl, csl].rearrange("(kc p) q -> p kc q", p=128)
        for half in range(2):
            em("sync", lambda e, ksrc=ksrc, up=up, half=half: e.dma_start(
                H["k1u8"][up][:, :].rearrange("p (kc q) -> p kc q", q=512)[
                    :, half * 4:(half + 1) * 4, :],
                ksrc[:, half * 4:(half + 1) * 4, :]),
               "qs", 16, writes=[f"k1u8_{up}h{half}"])
        em("scalar", lambda e, pp=pp, up=up: e.activation(
            H["w1b"][pp][:, :], H["k1u8"][up][:, :], AF.Copy,
            bias=-1.0, scale=2.0),
           "ae", 1, reads=[f"k1u8_{up}h0", f"k1u8_{up}h1"],
           writes=[f"w1b{pp}hA"])


    for t in range(NTILE):
        for c in range(NCH):
            g = t * NCH + c
            tA, tB = alloc_banks(2)
            em("tensor", lambda e, c=c, tA=tA, tB=tB: [
                e.matmul(H["pb"][(tA, tB)[kc // 4]][:, (kc % 4) * 128:(kc % 4 + 1) * 128],
                         H["xn"][c][:, kc * 128:(kc + 1) * 128],
                         H["ident"][:, :], start=True, stop=True)
                for kc in range(8)][-1],
               "pe", 1, reads=[f"xn{c}", "ident"], writes=[f"pb{tA}", f"pb{tB}"])
            em("scalar", lambda e, c=c, tA=tA: e.copy(
                xn_v[:, 0:4, c * 128:(c + 1) * 128],
                H["pb"][tA][:, :].rearrange("p (kc j) -> p kc j", j=128)),
               "ae", 1, reads=[f"pb{tA}"], writes=[f"xnfm{c}_lo"])
            em("vector", lambda e, c=c, tB=tB: e.tensor_copy(
                xn_v[:, 4:8, c * 128:(c + 1) * 128],
                H["pb"][tB][:, :].rearrange("p (kc j) -> p kc j", j=128)),
               "ve", 1, reads=[f"pb{tB}"], writes=[f"xnfm{c}"])
            bA, bB = alloc_banks(2)
            banks = (bA, bB)

            def attn_mms(e, c=c, banks=banks):
                last = None
                for h in range(2):
                    hs = slice(h * 512, (h + 1) * 512)
                    pb = H["pb"][banks[h]][:, :]
                    last = e.matmul(pb, H["ones_row"][:, :],
                                    H["prefbf"][:, hs], start=True, stop=False)
                    for sc in range(c):
                        last = e.matmul(pb, H["ones_sq"][:, :],
                                        H["xn"][sc][:, hs], start=False, stop=False)
                    last = e.matmul(pb, H["triu"][:, :],
                                    H["xn"][c][:, hs], start=False, stop=True)
                return last
            em("tensor", attn_mms, "pe", 1,
               reads=["ones_row", "ones_sq", "triu", "prefbf"]
                     + [f"xn{sc}" for sc in range(c + 1)],
               writes=[f"pb{bA}", f"pb{bB}"])
            cA, cB = alloc_banks(2)

            def colsum_mms(e, c=c, cA=cA, cB=cB):
                e.matmul(H["pb"][cA][0:1, :], H["ones_col"][:, :],
                         H["xn"][c][:, 0:512], start=True, stop=True)
                return e.matmul(H["pb"][cB][0:1, :], H["ones_col"][:, :],
                                H["xn"][c][:, 512:1024], start=True, stop=True)
            em("tensor", colsum_mms, "pe", 1,
               reads=["ones_col", f"xn{c}"], writes=[f"pb{cA}", f"pb{cB}"])
            for h, b in ((0, bA), (1, bB)):
                hs = slice(h * 512, (h + 1) * 512)
                em("vector", lambda e, c=c, b=b, hs=hs, g=g: e.scalar_tensor_tensor(
                    H["acc"][c][:, hs], H["pb"][b][:, :],
                    H["recip"][:, g:g + 1], H["xs"][c][:, hs],
                    op0=ALU.mult, op1=ALU.add),
                   "ve", 1, reads=[f"pb{b}", "recip", f"x{c}"],
                   writes=[f"acc{c}h{h}"])
            for h, b in ((0, cA), (1, cB)):
                hs = slice(h * 512, (h + 1) * 512)
                em("vector", lambda e, b=b, hs=hs: e.tensor_tensor(
                    H["pref32"][0:1, hs], H["pref32"][0:1, hs],
                    H["pb"][b][0:1, :], op=ALU.add),
                   "ve", 1, reads=[f"pb{b}", "pref32"], writes=["pref32"])
        em("vector", lambda e: e.tensor_copy(H["prefbf"][0:1, :], H["pref32"][0:1, :]),
           "ve", 1, reads=["pref32"], writes=["prefbf"])

        if t + 1 < NTILE:
            load_x(t + 1)

        preconverted = {(0, 0), (0, 1)}
        for mcg_, q_ in sorted(preconverted):
            emit_w1_piece(t, mcg_, q_)

        for mcg in range(8):
            pp = mcg % 2
            if t == 0:
                if mcg >= 2:
                    init_k0(mcg)
            elif mcg >= 2:
                em("sync", lambda e, mcg=mcg, pp=pp: e.dma_start(
                    H["cv"][pp][:, :].rearrange("p (kc q) -> p kc q", q=512),
                    D["w0s"][mcg].rearrange("kc p q -> p kc q")),
                   "qs", 16, reads=[f"scr_w0_{mcg}"],
                   writes=[f"cv{pp}s{j}" for j in range(8)])
            wsrc = H["cv"][pp]
            wreads = [f"cv{pp}s{j}" for j in range(8)]
            banks = alloc_banks(4)
            for ml in range(4):
                b = banks[ml]

                def mm0_bank(e, wsrc=wsrc, b=b, ml=ml):
                    last = None
                    for kc in range(8):
                        last = e.matmul(
                            H["pb"][b][:, :],
                            wsrc[:, kc * 512 + ml * 128: kc * 512 + (ml + 1) * 128],
                            xn_v[:, kc, :],
                            start=(kc == 0), stop=(kc == 7))
                    return last
                em("tensor", mm0_bank, "pe", 1,
                   reads=wreads + [f"xnfm{c}" for c in range(NCH)]
                         + [f"xnfm{c}_lo" for c in range(NCH)],
                   writes=[f"pb{b}"])
            for ml in range(4):
                mc = mcg * 4 + ml
                b = banks[ml]
                if t == 0:
                    em("scalar", lambda e, mc=mc, b=b: e.activation(
                        H["a1"][:, mc * TT:(mc + 1) * TT], H["pb"][b][:, :],
                        AF.Relu, scale=H["sreg"][:, 0:1]),
                       "ae", 1, reads=[f"pb{b}", "sreg"], writes=[f"a1_{mc}"])
                else:
                    em("vector", lambda e, mc=mc, b=b: e.tensor_scalar(
                        H["a1"][:, mc * TT:(mc + 1) * TT], H["pb"][b][:, :],
                        H["sreg"][:, 0:1], 0.0, op0=ALU.mult, op1=ALU.max),
                       "ve", 1, reads=[f"pb{b}", "sreg"], writes=[f"a1_{mc}"])


        for mcg in range(8):
            if t == 0 and mcg % 2 == 1:
                init_k2_unit(mcg // 2)
            banks = alloc_banks(4)
            for q in range(4):
                if (mcg, q) not in preconverted:
                    emit_w1_piece(t, mcg, q)
                pp = (t * 32 + mcg * 4 + q) % 4

                for ml in range(4):
                    b = banks[ml]

                    def mm1_bank(e, pp=pp, q=q, b=b, ml=ml):
                        last = None
                        for kcl in range(8):
                            kc = q * 8 + kcl
                            last = e.matmul(
                                H["pb"][b][:, :],
                                H["w1b"][pp][:, kcl * 512 + ml * 128: kcl * 512 + (ml + 1) * 128],
                                H["a1"][:, kc * TT:(kc + 1) * TT],
                                start=(kc == 0), stop=(kc == 31))
                        return last
                    em("tensor", mm1_bank, "pe", 1,
                       reads=[f"w1b{pp}hA"]
                             + [f"a1_{q * 8 + kcl}" for kcl in range(8)],
                       writes=[f"pb{b}"])
            for ml in range(4):
                mc = mcg * 4 + ml
                b = banks[ml]
                em("vector", lambda e, mc=mc, b=b: e.tensor_scalar(
                    H["a2"][:, mc * TT:(mc + 1) * TT], H["pb"][b][:, :],
                    H["sreg"][:, 1:2], 0.0, op0=ALU.mult, op1=ALU.max),
                   "ve", 1, reads=[f"pb{b}", "sreg"], writes=[f"a2_{mc}"])

        if t + 1 < NTILE:
            for mcg_ in range(2):
                em("sync", lambda e, mcg_=mcg_: e.dma_start(
                    H["cv"][mcg_][:, :].rearrange("p (kc q) -> p kc q", q=512),
                    D["w0s"][mcg_].rearrange("kc p q -> p kc q")),
                   "qs", 16, reads=[f"scr_w0_{mcg_}"],
                   writes=[f"cv{mcg_}s{j}" for j in range(8)])

        for h in range(2):
            if h == 1 and t + 1 < NTILE:
                ln_tile()
            banks = alloc_banks(4)
            for piece in range(8):
                pp = (h * 8 + piece) % 3
                em("sync", lambda e, h=h, piece=piece, pp=pp: e.dma_start(
                    H["w2b"][pp][:, :].rearrange("p (kc q) -> p kc q", q=512),
                    D["w2s"][h, piece * 4:(piece + 1) * 4].rearrange("kc p q -> p kc q")),
                   "qs", 16, reads=[f"scr_w2_{h}_{piece // 2}"],
                   writes=[f"w2b{pp}"])

                for tc in range(4):
                    b = banks[tc]

                    def mm2_bank(e, piece=piece, pp=pp, b=b, tc=tc):
                        last = None
                        for kcl in range(4):
                            kc = piece * 4 + kcl
                            last = e.matmul(
                                H["pb"][b][:, :],
                                H["a2"][:, kc * TT + tc * 128: kc * TT + (tc + 1) * 128],
                                H["w2b"][pp][:, kcl * 512:(kcl + 1) * 512],
                                start=(kc == 0), stop=(kc == 31))
                        return last
                    em("tensor", mm2_bank, "pe", 1,
                       reads=[f"w2b{pp}"] + [f"a2_{piece * 4 + kcl}" for kcl in range(4)],
                       writes=[f"pb{b}"])
            hs = slice(h * 512, (h + 1) * 512)
            for tc in range(4):
                b = banks[tc]
                em("vector", lambda e, tc=tc, b=b, hs=hs: e.scalar_tensor_tensor(
                    H["acc"][tc][:, hs], H["pb"][b][:, :],
                    H["sreg"][:, 2:3], H["acc"][tc][:, hs],
                    op0=ALU.mult, op1=ALU.add),
                   "ve", 1, reads=[f"pb{b}", "sreg", f"acc{tc}h{h}"],
                   writes=[f"acc{tc}h{h}", f"out{tc}h{h}"])
            for c in range(NCH):
                row0 = (t * NCH + c) * 128
                em("gpsimd", lambda e, c=c, row0=row0, hs=hs: e.dma_start(
                    D["y"][row0:row0 + 128, hs], H["acc"][c][:, hs]),
                   "qg", 16, reads=[f"out{c}h{h}", f"acc{c}h{h}"],
                   writes=[f"ydone{t}_{c}h{h}"])

    if _DEBUG:
        for nm, src, deps in [
            ("dbg_w0s", D["w0s"], [f"scr_w0_{g}" for g in range(8)]),
            ("dbg_w2s", D["w2s"], [f"scr_w2_{h}_{q}" for h in range(2) for q in range(4)]),
        ]:
            em("gpsimd", lambda e, nm=nm, src=src: e.dma_start(
                D["dbg"][nm][:, :, :, :], src[:, :, :, :]),
               "qg", 16, reads=deps)
        em("gpsimd", lambda e: e.dma_start(D["dbg"]["dbg_a1"][:, :], H["a1"][:, :]),
           "qg", 16, reads=[f"a1_{mc}" for mc in range(32)])
        em("gpsimd", lambda e: e.dma_start(D["dbg"]["dbg_a2"][:, :], H["a2"][:, :]),
           "qg", 16, reads=[f"a2_{mc}" for mc in range(32)])
        em("gpsimd", lambda e: e.dma_start(D["dbg"]["dbg_xnfm"][:, :], H["xnfm"][:, :]),
           "qg", 16, reads=[f"xnfm{c}" for c in range(NCH)])
        em("gpsimd", lambda e: e.dma_start(D["dbg"]["dbg_sreg"][:, :], H["sreg"][:, :]),
           "qg", 16, reads=["sreg"])

    if ctx.role == "gpsimd":
        for i in range(NPOOL):
            if ctx.cnt[f"qg{i}"] > 0:
                ctx.eng.wait_ge(ctx.sems[f"qg{i}"], ctx.cnt[f"qg{i}"])


_consts_cache = {}


def _make_consts():
    if _consts_cache:
        return _consts_cache
    _consts_cache.update(
        ident=np.eye(128, dtype=ml_dtypes.bfloat16),
        triu=np.triu(np.ones((128, 128))).astype(ml_dtypes.bfloat16),
        ones_sq=np.ones((128, 128), dtype=ml_dtypes.bfloat16),
        ones_row=np.ones((1, 128), dtype=ml_dtypes.bfloat16),
        ones_col=np.ones((128, 1), dtype=ml_dtypes.bfloat16),
        ones_f32=np.ones((1, 128), dtype=np.float32),
    )
    return _consts_cache


def kernel(L0_attn_x, L0_ffn_k0, L0_ffn_k1, L0_ffn_k2, L0_ffn_s0, L0_ffn_s1,
           L0_ffn_s2):
    x = np.ascontiguousarray(np.asarray(L0_attn_x, dtype=np.float32))
    k0 = np.ascontiguousarray(np.asarray(L0_ffn_k0)).view(np.uint8)
    k1 = np.ascontiguousarray(np.asarray(L0_ffn_k1)).view(np.uint8)
    k2 = np.ascontiguousarray(np.asarray(L0_ffn_k2)).view(np.uint8)
    svals = np.ascontiguousarray(
        np.stack([np.asarray(L0_ffn_s0), np.asarray(L0_ffn_s1),
                  np.asarray(L0_ffn_s2)]).astype(np.float32).reshape(1, 3))
    cn = _make_consts()

    nc = build_nc()
    in_maps = []
    for b in range(8):
        in_maps.append(dict(
            x=np.ascontiguousarray(x[b]), k0=k0, k1=k1, k2=k2, svals=svals,
            ident=cn["ident"], triu=cn["triu"], ones_sq=cn["ones_sq"],
            ones_row=cn["ones_row"], ones_col=cn["ones_col"],
            ones_f32=cn["ones_f32"]))
        if _DEBUG_W:
            in_maps[-1].update(_DEBUG_W)
    res = run_bass_kernel_spmd(nc, in_maps, core_ids=list(range(8)))
    global _LAST
    _LAST = res
    return np.stack([np.asarray(res.results[b]["y"]) for b in range(8)], axis=0)


_LAST = None
